# version 24 (final)
# baseline (speedup 1.0000x reference)
import numpy as np
import ml_dtypes
from contextlib import ExitStack

import concourse.bass as bass
import concourse.mybir as mybir
from concourse.bass_utils import run_bass_kernel_spmd

F32 = mybir.dt.float32
BF16 = mybir.dt.bfloat16
I32 = mybir.dt.int32
AF = mybir.ActivationFunctionType
OP = mybir.AluOpType

NCORE = 8
SEQ = 2
T = 2048
DM = 1024
NH = 16
ST = 512
NST = T // ST
NCH = ST // 128
NJ = DM // 128
NMI_F = 3072 // 128
NKT = 2 * NJ
S127_2 = float(np.float32(1.0) / (np.float32(127.0) * np.float32(127.0)))

N_SUBTILES = SEQ * NST
N_SRC_STREAM = 0

CV = {
    "neg1": (0, -1.0),
    "neghalf": (1, -0.5),
    "eps_ln": (2, 1e-5),
    "inv_dm": (3, 1.0 / 1024.0),
    "inv127": (4, float(np.float32(1.0) / np.float32(127.0))),
    "neg_eps8": (5, -1e-8),
}

ENGINES = ("sync", "vector", "scalar", "gpsimd", "tensor")


class Ch:
    def __init__(self, sem, name):
        self.sem = sem
        self.name = name
        self.n = 0


class Prog:
    def __init__(self):
        self.items = {e: [] for e in ENGINES}

    def op(self, eng, fn, waits=(), inc=None, inc_n=1):
        w = [(ch, v) for ch, v in waits if v > 0]
        post = None
        if inc is not None:
            inc.n += inc_n
            post = (inc, inc_n)
        self.items[eng].append((fn, w, post))
        return inc.n if inc is not None else None

    def replay(self, block):
        for eng_name in ENGINES:
            items = self.items[eng_name]

            def make(eng_items):
                def body(eng):
                    seen = {}
                    for fn, waits, post in eng_items:
                        for ch, v in waits:
                            if seen.get(ch, 0) >= v:
                                continue
                            eng.wait_ge(ch.sem, v)
                            seen[ch] = v
                        ins = fn(eng)
                        if post is not None:
                            ch, k = post
                            ins.then_inc(ch.sem, k)
                return body

            getattr(block, eng_name)(make(items))


def build_nc():
    nc = bass.Bass("TRN2", target_bir_lowering=False)

    x_in = nc.dram_tensor("x", [SEQ, T, DM], F32, kind="ExternalInput").ap()
    wq_in = nc.dram_tensor("wq", [3 * NJ, 128, NJ * 128], F32,
                           kind="ExternalInput").ap()
    wo_in = nc.dram_tensor("wo", [NKT, 128, DM], F32, kind="ExternalInput").ap()
    wg_in = nc.dram_tensor("wg", [NMI_F, 128, NJ * 128], I32,
                           kind="ExternalInput").ap()
    wu_in = nc.dram_tensor("wu", [NMI_F, 128, NJ * 128], I32,
                           kind="ExternalInput").ap()
    wd_in = nc.dram_tensor("wd", [NMI_F, 128, DM], I32,
                           kind="ExternalInput").ap()
    ab_in = nc.dram_tensor("ab", [1, 2 * NH], F32, kind="ExternalInput").ap()
    cv_in = nc.dram_tensor("cv", [128, 8], F32, kind="ExternalInput").ap()
    bones_in = nc.dram_tensor("bones", [128, 128], BF16,
                              kind="ExternalInput").ap()
    ones1_in = nc.dram_tensor("ones1", [1, 128], F32, kind="ExternalInput").ap()
    out_d = nc.dram_tensor("out", [SEQ, T, DM], F32, kind="ExternalOutput").ap()

    wq_s = nc.dram_tensor("wq_s", [3 * NJ, 128, NJ * 128], BF16,
                          kind="Internal").ap()
    wg_s = nc.dram_tensor("wg_s", [NMI_F, 128, NJ * 128], BF16,
                          kind="Internal").ap()
    wu_s = nc.dram_tensor("wu_s", [NMI_F, 128, NJ * 128], BF16,
                          kind="Internal").ap()
    wo_s = nc.dram_tensor("wo_s", [NKT, 128, DM], BF16, kind="Internal").ap()
    wd_s = nc.dram_tensor("wd_s", [NMI_F, 128, DM], BF16, kind="Internal").ap()
    xbf_d = nc.dram_tensor("xbf", [SEQ, T, DM], BF16, kind="Internal").ap()

    ctx = ExitStack()
    e = ctx.enter_context

    cv_sb = e(nc.sbuf_tensor("cv_sb", [128, 8], F32))
    bones_sb = e(nc.sbuf_tensor("bones_sb", [128, 128], BF16))
    ones1_sb = e(nc.sbuf_tensor("ones1_sb", [1, 128], F32))
    ab_sb = e(nc.sbuf_tensor("ab_sb", [1, 2 * NH], F32))
    dtmp_sb = e(nc.sbuf_tensor("dtmp_sb", [1, 2 * NH], F32))
    dbc_sb = e(nc.sbuf_tensor("dbc_sb", [128, 2 * NH], F32))
    dcols_sb = e(nc.sbuf_tensor("dcols_sb", [128, 16], F32))
    states_sb = e(nc.sbuf_tensor("states_sb", [128, 32], F32))

    wq_buf = e(nc.sbuf_tensor("wq_buf", [128, 3 * 2048], BF16))
    wg_buf = e(nc.sbuf_tensor("wg_buf", [128, 3 * 2048], BF16))
    wu_buf = e(nc.sbuf_tensor("wu_buf", [128, 3 * 2048], BF16))
    wo_buf = e(nc.sbuf_tensor("wo_buf", [128, 6 * 512], BF16))
    wd_buf = e(nc.sbuf_tensor("wd_buf", [128, 6 * 512], BF16))


    xbm = e(nc.sbuf_tensor("xbm", [128, NCH * DM], BF16))
    xT = e(nc.sbuf_tensor("xT", [128, NJ * ST], BF16))
    work = [e(nc.sbuf_tensor(f"work{i}", [128, NCH * DM], F32))
            for i in range(2)]
    ebuf = e(nc.sbuf_tensor("ebuf", [128, 1024], F32))
    kphi = e(nc.sbuf_tensor("kphi", [128, 1024], F32))
    kvb = e(nc.sbuf_tensor("kvb", [128, 1024], F32))
    qb = [e(nc.sbuf_tensor(f"qb{i}", [128, 1024], F32)) for i in range(2)]
    runs = e(nc.sbuf_tensor("runs", [128, 8 * ST], F32))
    denr = e(nc.sbuf_tensor("denr", [128, 1024], BF16))
    denc = e(nc.sbuf_tensor("denc", [128, 1024], F32))
    concat = e(nc.sbuf_tensor("concat", [128, NKT * ST], BF16))
    y1T = e(nc.sbuf_tensor("y1T", [128, NJ * ST], BF16))
    y1b = e(nc.sbuf_tensor("y1b", [128, NCH * DM], BF16))
    sgb = [e(nc.sbuf_tensor(f"sgb{i}", [128, 2 * ST], BF16)) for i in range(2)]
    hbuf = e(nc.sbuf_tensor("hbuf", [128, NMI_F * ST], BF16))
    stat = e(nc.sbuf_tensor("stat", [128, 48], F32))

    for name, (idx, val) in CV.items():
        nc.const_aps.register(F32, val, cv_sb[:, idx:idx + 1])

    A_ps = e(nc.psum_tensor("A_ps", [128, 1024], F32))
    B_ps = e(nc.psum_tensor("B_ps", [128, 1024], F32))
    C_ps = e(nc.psum_tensor("C_ps", [128, 1024], F32))
    D_ps = e(nc.psum_tensor("D_ps", [128, 1024], F32))

    def mkch(name):
        return Ch(e(nc.semaphore(name)), name)

    ch_v = mkch("ch_v")
    ch_s = mkch("ch_s")
    ch_p = mkch("ch_p")
    ch_m = mkch("ch_m")
    ch_ld = mkch("ch_ld")
    ch_xc = mkch("ch_xc")
    ch_xt = mkch("ch_xt")
    ch_yt = mkch("ch_yt")
    ch_w = {key: [mkch(f"ch_w_{key}{i}") for i in range(6 if key in ("wo", "wd") else 3)]
            for key in ("wq", "wg", "wu", "wo", "wd")}
    ch_wq_g = [mkch(f"ch_wq_g{i}") for i in range(3)]
    ch_od2 = [mkch("ch_od0"), mkch("ch_od1")]

    prog = Prog()

    def cvap(name, parts=128):
        return cv_sb[0:parts, CV[name][0]:CV[name][0] + 1]

    def ld(dst, src):
        prog.op("sync", lambda s, d=dst, z=src: s.dma_start(out=d, in_=z),
                inc=ch_ld, inc_n=16)

    ld(cv_sb[:], cv_in)
    ld(bones_sb[:], bones_in)
    ld(ones1_sb[:], ones1_in)
    ld(ab_sb[:], ab_in)
    lds_done = ch_ld.n

    v_sig = prog.op("scalar",
                    lambda s: s.activation(dtmp_sb[:], ab_sb[:], AF.Sigmoid),
                    waits=[(ch_ld, lds_done)], inc=ch_s)
    v_rel = prog.op("scalar",
                    lambda s: s.activation(dtmp_sb[:], dtmp_sb[:], AF.Relu,
                                           bias=cvap("neg_eps8", 1)),
                    waits=[(ch_s, v_sig)], inc=ch_s)
    v_add = prog.op("vector",
                    lambda v: v.tensor_scalar(dtmp_sb[:], dtmp_sb[:], 1e-8,
                                              None, OP.add),
                    waits=[(ch_s, v_rel)], inc=ch_v)
    v_bc = prog.op("tensor",
                   lambda t: t.matmul(A_ps[:, 0:2 * NH], ones1_sb[:],
                                      dtmp_sb[:], start=True, stop=True),
                   waits=[(ch_v, v_add), (ch_ld, lds_done)], inc=ch_m)
    v_evac = prog.op("vector",
                     lambda v: v.tensor_copy(dbc_sb[:], A_ps[:, 0:2 * NH]),
                     waits=[(ch_m, v_bc)], inc=ch_v)
    for j in range(NJ):
        for dec in range(2):
            src0 = dbc_sb[0:64, dec * NH + 2 * j: dec * NH + 2 * j + 1]
            src1 = dbc_sb[64:128, dec * NH + 2 * j + 1: dec * NH + 2 * j + 2]
            dst0 = dcols_sb[0:64, dec * 8 + j: dec * 8 + j + 1]
            dst1 = dcols_sb[64:128, dec * 8 + j: dec * 8 + j + 1]
            prog.op("vector", lambda v, a=dst0, b=src0: v.tensor_copy(a, b),
                    waits=[(ch_v, v_evac)], inc=ch_v)
            prog.op("vector", lambda v, a=dst1, b=src1: v.tensor_copy(a, b),
                    waits=[(ch_v, v_evac)], inc=ch_v)
    dcols_ready = ch_v.n

    conv_ready = {"wq": [None] * (3 * NJ), "wo": [None] * NKT,
                  "wg": [None] * NMI_F, "wu": [None] * NMI_F,
                  "wd": [None] * NMI_F, "x": [None] * N_SUBTILES}

    def chof(tag):
        return {"v": ch_v, "s": ch_s, "p": ch_p, "m": ch_m}[tag]

    def emit_conversions():
        def xconv(sg, gated):
            st_, stt = divmod(sg, NST)
            ch = mkch(f"cc_x_{sg}")
            for c in range(NCH):
                r0 = stt * ST + c * 128
                prog.op("gpsimd",
                        lambda g, d=xbf_d[st_, r0:r0 + 128, :],
                        z=x_in[st_, r0:r0 + 128, :]: g.dma_start(out=d, in_=z),
                        waits=([(ch_xt, 4 * 16)] if gated else []),
                        inc=ch, inc_n=16)
            conv_ready["x"][sg] = (ch, NCH * 16)

        xconv(0, False)
        for c in range(NCH):
            prog.op("gpsimd",
                    lambda g, d=xbm[:, c * DM:(c + 1) * DM],
                    z=xbf_d[0, c * 128:(c + 1) * 128, :]:
                        g.dma_start(out=d, in_=z),
                    waits=[conv_ready["x"][0]], inc=ch_xc, inc_n=16)
        xcast_done[0] = ch_xc.n

        srcs = {"wq": (wq_in, wq_s), "wo": (wo_in, wo_s),
                "wg": (wg_in, wg_s), "wu": (wu_in, wu_s),
                "wd": (wd_in, wd_s)}

        def conv_chunk(key, c0, csz=4):
            src, dst = srcs[key]
            c1 = c0 + csz
            ch = mkch(f"cc_{key}_{c0}")
            for mi in range(c0, c1):
                prog.op("gpsimd",
                        lambda g, d=dst[mi], z=src[mi]:
                            g.dma_start(out=d, in_=z),
                        waits=[(ch_xt, 4 * 16)], inc=ch, inc_n=16)
            for mi in range(c0, c1):
                conv_ready[key][mi] = (ch, (c1 - c0) * 16)

        order = [("wq", 0), ("wq", 4), ("wq", 8), ("wq", 12), ("x", 1),
                 ("wq", 16), ("wq", 20), ("wo", 0), ("wo", 4), ("x", 2),
                 ("wo", 8), ("wo", 12), ("wg", 0), ("wu", 0),
                 ("wg", 4), ("wu", 4), ("x", 3), ("wg", 8), ("wu", 8),
                 ("wd", 0), ("wd", 4), ("x", 4), ("wg", 12), ("wu", 12),
                 ("wd", 8), ("x", 5), ("wg", 16), ("wu", 16),
                 ("wd", 12), ("x", 6), ("wg", 20), ("wu", 20),
                 ("wd", 16), ("wd", 20), ("x", 7)]
        for key, c0 in order:
            if key == "x":
                if c0 < N_SUBTILES:
                    xconv(c0, True)
            else:
                conv_chunk(key, c0)

    wq_cval = {}
    wg_cval = {}
    wu_cval = {}
    wo_cval = {}
    wd_cval = {}
    wq_slot_g = [0, 0, 0]
    wg_slot_g = [0, 0, 0]
    wu_slot_g = [0, 0, 0]
    wo_slot_g = [0] * 6
    wd_slot_g = [0] * 6

    def stream_chunk2(sg, key, buf, scratch, src4, slot_g, cval, ci,
                      q="sync"):
        sl = ci % 3
        wch = ch_wq_g[sl] if (key == "wq" and q == "gpsimd") else ch_w[key][sl]
        mi0 = 2 * ci
        rdy = [conv_ready[key][mi0], conv_ready[key][mi0 + 1]]
        assert all(r is not None for r in rdy), (key, mi0)
        for a in range(2):
            dst2 = buf[:, sl * 2048 + a * 1024: sl * 2048 + (a + 1) * 1024]
            prog.op(q,
                    lambda s, d=dst2, z=scratch[mi0 + a]:
                        s.dma_start(out=d, in_=z),
                    waits=rdy + [(ch_m, slot_g[sl])],
                    inc=wch, inc_n=16)
        cval[ci] = (wch, wch.n)

    def stream_half(sg, key, buf, scratch, src3, slot_g, cval, k, h2, ridx):
        sl = ridx % 6
        wch = ch_w[key][sl]
        dst = buf[:, sl * 512:(sl + 1) * 512]
        if sg < N_SRC_STREAM:
            src = src3[k][:, h2 * 512:(h2 + 1) * 512]
            prog.op("gpsimd",
                    lambda g, d=dst, z=src: g.dma_start(out=d, in_=z),
                    waits=[(ch_m, slot_g[sl])], inc=wch, inc_n=16)
        else:
            src = scratch[k][:, h2 * 512:(h2 + 1) * 512]
            rdy = conv_ready[key][k]
            assert rdy is not None, (key, k)
            prog.op("sync",
                    lambda s, d=dst, z=src: s.dma_start(out=d, in_=z),
                    waits=[rdy, (ch_m, slot_g[sl])],
                    inc=wch, inc_n=16)
        cval[(k, h2)] = (wch, wch.n)

    g = dict(xbm_v=0, xbm_yt=0, xT=0, y1T=0, concat=0, h=0, runs=0,
             kphi=0, kvb=0, eb_r=0, denr=0, denc=0, stat_v=0, stat_s=0,
             runs_p=0, concat_ready=0, eb_s=0)
    g_stat = [0, 0]
    g_q = [0, 0]
    g_sg = [0, 0]
    g_work = [0, 0]
    gPS = {"A": [("v", v_evac)], "B": [("v", 0)],
           "C": [("v", 0)], "D": [("v", 0)]}

    n_out_dma = [0]
    xcast_done = {}

    def emit_xcast(sg):
        if sg in xcast_done:
            return xcast_done[sg]
        s, st = divmod(sg, NST)
        for c in range(NCH):
            xsrc = xbf_d[s, st * ST + c * 128: st * ST + (c + 1) * 128, :]
            prog.op("gpsimd",
                    lambda gg, z=xsrc, d=xbm[:, c * DM:(c + 1) * DM]:
                        gg.dma_start(out=d, in_=z),
                    waits=[(ch_v, g["xbm_v"]), conv_ready["x"][sg]],
                    inc=ch_xc, inc_n=16)
        xcast_done[sg] = ch_xc.n
        return ch_xc.n

    def emit_den(pr, qp, v_q, v_scans, num_vals):
        for jj in range(2):
            j = 2 * pr + jj
            qv = qb[qp][:, jj * 512:(jj + 1) * 512].rearrange(
                "p (o t) -> p o t", o=1).broadcast_to([128, 2, 512])
            kvpart = runs[:, jj * 2048: jj * 2048 + 1024]
            kspart = runs[:, jj * 2048 + 1024: jj * 2048 + 2048]
            v_dr = prog.op(
                "gpsimd",
                lambda v, q_=qv, r_=kspart: v.tensor_tensor(
                    denr.ap().rearrange("p (o t) -> p o t", o=2), q_,
                    r_.rearrange("p (o t) -> p o t", o=2), OP.mult),
                waits=[(ch_v, v_scans), (ch_v, v_q),
                       (ch_m, g["denr"])], inc=ch_p)
            wB2 = [(chof(t), v) for t, v in gPS["B"]]
            m_d = None
            for dec in range(2):
                m_d = prog.op(
                    "tensor",
                    lambda t, o=B_ps[:, dec * 512:(dec + 1) * 512],
                    r_=denr[:, dec * 512:(dec + 1) * 512]:
                        t.matmul(o, bones_sb[:], r_, start=True,
                                 stop=True),
                    waits=([(ch_p, v_dr)] + (wB2 if dec == 0 else [])),
                    inc=ch_m if dec == 1 else None, inc_n=2)
            g["denr"] = ch_m.n
            v_cl = prog.op("vector",
                           lambda v: v.tensor_scalar(
                               denc[:], B_ps[:], 1e-6, None, OP.max),
                           waits=[(ch_m, m_d), (ch_v, g["denc"])],
                           inc=ch_v)
            gPS["B"] = [("v", v_cl)]
            v_ln = prog.op("scalar",
                           lambda s_: s_.activation(denc[:], denc[:],
                                                    AF.Ln),
                           waits=[(ch_v, v_cl)], inc=ch_s)
            v_ex = prog.op("scalar",
                           lambda s_: s_.activation(
                               denc[:], denc[:], AF.Exp,
                               scale=cvap("neg1")),
                           waits=[(ch_s, v_ln)], inc=ch_s)
            num_val = num_vals[jj]
            cview = concat.ap().rearrange(
                "p (d e t) -> p d e t", d=2, e=NJ)[:, :, j, :]
            v_of = prog.op(
                "vector",
                lambda v, o=cview, r_=kvpart: v.tensor_tensor(
                    o, r_.rearrange("p (o t) -> p o t", o=2),
                    denc.ap().rearrange("p (o t) -> p o t", o=2),
                    OP.mult),
                waits=[(ch_s, v_ex), (ch_m, g["concat"]),
                       (ch_p, num_val)],
                inc=ch_v)
            g["denc"] = v_of
            g["concat_ready"] = v_of
        g["runs"] = ch_v.n
        g_q[qp] = ch_v.n

    den_q = []

    def pop_den():
        if den_q:
            emit_den(*den_q.pop(0))

    def emit_A(sg):
        s, st = divmod(sg, NST)
        wq_q = "gpsimd" if sg >= 3 else "scalar"
        for ci in range(3):
            stream_chunk2(sg, "wq", wq_buf, wq_s, wq_in, wq_slot_g,
                          wq_cval, ci, q=wq_q)
        emit_xcast(sg)
        for c in range(NCH):
            dview = xT.ap().rearrange("p (j t) -> p j t", j=NJ)[
                :, :, c * 128:(c + 1) * 128]
            sview = xbf_d[s, st * ST + c * 128: st * ST + (c + 1) * 128,
                          :].rearrange("p (j f) -> p j f", j=NJ)
            prog.op("sync",
                    lambda s_, d=dview, z=sview: s_.dma_start(
                        out=d, in_=z, transpose=True),
                    waits=[conv_ready["x"][sg], (ch_m, g["xT"])],
                    inc=ch_xt, inc_n=16)
        xt_val = ch_xt.n

        def qkv_mm(mi, ps, col_off, extra_waits):
            ci = mi // 2
            slot = ci % 3
            woff = slot * 2048 + (mi % 2) * 1024
            last = None
            for ki in range(NJ):
                lhs = wq_buf[:, woff + ki * 128: woff + (ki + 1) * 128]
                rhs = xT[:, ki * ST:(ki + 1) * ST]
                w = [wq_cval[ci], (ch_xt, xt_val)]
                if ki == 0:
                    w = w + extra_waits
                fin = (ki == NJ - 1)
                last = prog.op(
                    "tensor",
                    lambda t, o=ps[:, col_off:col_off + 512], a=lhs, b=rhs,
                    f1=(ki == 0), f2=fin:
                        t.matmul(o, a, b, start=f1, stop=f2),
                    waits=w, inc=ch_m if fin else None, inc_n=NJ)
            wq_slot_g[slot] = ch_m.n
            if mi % 2 == 1 and ci + 3 < 12:
                stream_chunk2(sg, "wq", wq_buf, wq_s, wq_in, wq_slot_g,
                              wq_cval, ci + 3, q=wq_q)
            return last

        for pr in range(NJ // 2):
            qp = pr % 2
            base = pr * 6
            wA = [(chof(t), v) for t, v in gPS["A"]]
            qkv_mm(base + 0, A_ps, 0, wA)
            m_k = qkv_mm(base + 1, A_ps, 512, [])
            v_e = prog.op("scalar",
                          lambda s_: s_.activation(ebuf[:], A_ps[:], AF.Exp),
                          waits=[(ch_m, m_k), (ch_v, g["eb_r"]),
                                 (ch_s, g["eb_s"])], inc=ch_s)
            g["eb_s"] = v_e
            v_r = prog.op("scalar",
                          lambda s_: s_.activation(kphi[:], A_ps[:], AF.Relu),
                          waits=[(ch_m, m_k), (ch_v, g["kphi"])], inc=ch_s)
            v_kphi = prog.op("vector",
                             lambda v: v.scalar_tensor_tensor(
                                 kphi[:], ebuf[:], 1.0, kphi[:], OP.min,
                                 OP.add),
                             waits=[(ch_s, v_e), (ch_s, v_r)], inc=ch_v)
            gPS["A"] = [("s", v_e), ("s", v_r)]
            g["eb_r"] = v_kphi

            wB = [(chof(t), v) for t, v in gPS["B"]]
            qkv_mm(base + 2, B_ps, 0, wB)
            m_v = qkv_mm(base + 3, B_ps, 512, [])
            v_kv = prog.op("vector",
                           lambda v: v.tensor_tensor(kvb[:], kphi[:],
                                                     B_ps[:], OP.mult),
                           waits=[(ch_m, m_v), (ch_v, v_kphi),
                                  (ch_v, g["kvb"])], inc=ch_v)
            gPS["B"] = [("v", v_kv)]

            m_q0 = qkv_mm(base + 4, A_ps, 0,
                          [(ch_v, v_kphi), (ch_s, v_r), (ch_s, v_e)])
            m_q = qkv_mm(base + 5, A_ps, 512, [])
            v_eq = prog.op("scalar",
                           lambda s_: s_.activation(ebuf[:], A_ps[:], AF.Exp),
                           waits=[(ch_m, m_q), (ch_v, v_kphi),
                                  (ch_v, g["eb_r"]), (ch_s, g["eb_s"])],
                           inc=ch_s)
            g["eb_s"] = v_eq
            v_rq = prog.op("scalar",
                           lambda s_, d=qb[qp]: s_.activation(
                               d[:], A_ps[:], AF.Relu),
                           waits=[(ch_m, m_q), (ch_v, g_q[qp])], inc=ch_s)
            v_q = prog.op("vector",
                          lambda v, d=qb[qp]: v.scalar_tensor_tensor(
                              d[:], ebuf[:], 1.0, d[:], OP.min, OP.add),
                          waits=[(ch_s, v_eq), (ch_s, v_rq)], inc=ch_v)
            gPS["A"] = [("s", v_eq), ("s", v_rq)]
            g["eb_r"] = v_q

            yield

            for kind in (1, 0):
                for jj in range(2):
                    j = 2 * pr + jj
                    srcbuf = kvb if kind == 0 else kphi
                    src = srcbuf[:, jj * 512:(jj + 1) * 512]
                    src_val = v_kv if kind == 0 else v_kphi
                    for dec in range(2):
                        off = jj * 2048 + (kind * 2 + dec) * 512
                        sidx = j * 4 + kind * 2 + dec
                        dc = dcols_sb[:, dec * 8 + j: dec * 8 + j + 1]
                        init = 0.0 if st == 0 else states_sb[:, sidx:sidx + 1]
                        v_sc = prog.op(
                            "vector",
                            lambda v, o=runs[:, off:off + 512], d=dc, z=src,
                            ii=init: v.tensor_tensor_scan(
                                o, d.broadcast_to([128, 512]), z, ii,
                                OP.mult, OP.add),
                            waits=[(ch_v, src_val), (ch_v, dcols_ready),
                                   (ch_v, g["runs"])],
                            inc=ch_v)
            prog.op("vector",
                    lambda v, d=states_sb[:, 8 * pr: 8 * pr + 8],
                    z=runs.ap().rearrange("p (n t) -> p n t", n=8)[:, :, 511]:
                        v.tensor_copy(d, z),
                    waits=[(ch_v, ch_v.n)], inc=ch_v)
            v_scans = ch_v.n
            g["kphi"] = v_scans
            g["kvb"] = v_scans

            num_vals = []
            for jj in range(2):
                qv = qb[qp][:, jj * 512:(jj + 1) * 512].rearrange(
                    "p (o t) -> p o t", o=1).broadcast_to([128, 2, 512])
                kvpart = runs[:, jj * 2048: jj * 2048 + 1024]
                prog.op(
                    "gpsimd",
                    lambda v, q_=qv, r_=kvpart: v.tensor_tensor(
                        r_.rearrange("p (o t) -> p o t", o=2), q_,
                        r_.rearrange("p (o t) -> p o t", o=2), OP.mult),
                    waits=[(ch_v, v_scans), (ch_v, v_q)], inc=ch_p)
                num_vals.append(ch_p.n)
            den_q.append((pr, qp, v_q, v_scans, num_vals))
            yield
        g["xT"] = ch_m.n

    def run_gen(gen):
        if gen is None:
            return
        for _ in gen:
            pass

    def emit_BCD(sg, a_next_gen):
        s, st = divmod(sg, NST)
        wl = sg % 2
        so_ = wl * 24
        concat_val = g["concat_ready"]
        y1vals = {}
        h_vals = {}

        firstAB = [(ch_v, concat_val)] + [
            (chof(t), v) for key in ("A", "B") for t, v in gPS[key]]
        firstCD = [(chof(t), v) for key in ("C", "D") for t, v in gPS[key]]
        bank = (C_ps, D_ps, A_ps, B_ps)

        def lhs_wout(kt, c):
            return concat[:, kt * ST + c * 128: kt * ST + (c + 1) * 128]

        for k in range(3):
            for h2 in range(2):
                stream_half(sg, "wo", wo_buf, wo_s, wo_in, wo_slot_g,
                            wo_cval, k, h2, k * 2 + h2)
        for kt in range(NKT):
            for h2 in range(2):
                ridx = (kt * 2 + h2) % 6
                for ci, c in enumerate((2, 3, 0, 1)):
                    w = [wo_cval[(kt, h2)]]
                    if kt == 0 and h2 == 0:
                        if c == 2:
                            w = w + firstAB
                        elif c == 0:
                            w = w + firstCD
                    prog.op(
                        "tensor",
                        lambda t, o=bank[c][:, h2 * 512:(h2 + 1) * 512],
                        a=lhs_wout(kt, c),
                        b=wo_buf[:, ridx * 512:(ridx + 1) * 512],
                        f1=(kt == 0), f2=(kt == NKT - 1):
                            t.matmul(o, a, b, start=f1, stop=f2),
                        waits=w, inc=ch_m if ci == NCH - 1 else None,
                        inc_n=NCH)
                wo_slot_g[ridx] = ch_m.n
            if kt + 3 < NKT:
                for h2 in range(2):
                    stream_half(sg, "wo", wo_buf, wo_s, wo_in, wo_slot_g,
                                wo_cval, kt + 3, h2, (kt + 3) * 2 + h2)
        m_wave = ch_m.n
        gb = {}
        for c in (2, 3, 0, 1):
            for h2 in range(2):
                prog.op(
                    "vector",
                    lambda v, o=work[wl][:, c * DM + h2 * 512:
                                         c * DM + h2 * 512 + 512],
                    p_=bank[c][:, h2 * 512:(h2 + 1) * 512],
                    xx=xbm[:, c * DM + h2 * 512: c * DM + h2 * 512 + 512],
                    a_=stat[:, so_ + c * 2 + h2: so_ + c * 2 + h2 + 1]:
                        v.scalar_tensor_tensor(
                            o, p_, 1.0, xx, OP.mult, OP.add, accum_out=a_),
                    waits=[(ch_m, m_wave), (ch_od2[wl], g_work[wl]),
                           (ch_v, g_stat[wl]), (ch_xc, xcast_done[sg])],
                    inc=ch_v)
            gb[c] = ch_v.n
        gPS["A"] = [("v", gb[2])]
        gPS["B"] = [("v", gb[3])]
        gPS["C"] = [("v", gb[0])]
        gPS["D"] = [("v", gb[1])]
        g["concat"] = ch_m.n
        g["xbm_v"] = ch_v.n

        sview = stat[:, so_:so_ + 8].rearrange("p (c u) -> p c u", u=2)
        for grp in ((2, 3), (0, 1)):
            c0g = grp[0]
            v_s1b = prog.op(
                "vector",
                lambda v, d=stat[:, so_ + 8 + c0g:so_ + 10 + c0g],
                a=sview[:, c0g:c0g + 2, 0], b=sview[:, c0g:c0g + 2, 1]:
                    v.tensor_tensor(d, a, b, OP.add),
                waits=[(ch_v, gb[grp[1]])], inc=ch_v)
            v_nmb = prog.op(
                "vector",
                lambda v, d=stat[:, so_ + 8 + c0g:so_ + 10 + c0g]:
                    v.tensor_scalar(d, d, -1.0 / DM, None, OP.mult),
                waits=[(ch_v, v_s1b)], inc=ch_v)
            sq_last = None
            for c in grp:
                sq_last = prog.op(
                    "scalar",
                    lambda s_, z=work[wl][:, c * DM:(c + 1) * DM],
                    b=stat[:, so_ + 8 + c:so_ + 9 + c],
                    a_=stat[:, so_ + 12 + c:so_ + 13 + c]:
                        s_.activation(ebuf[:], z, AF.Square, bias=b,
                                      accum_out=a_),
                    waits=[(ch_v, v_nmb), (ch_v, g["eb_r"]),
                           (ch_s, g["eb_s"])],
                    inc=ch_s)
                g["eb_s"] = sq_last
            v_lnv = prog.op(
                "scalar",
                lambda s_, d=stat[:, so_ + 16 + c0g:so_ + 18 + c0g],
                z=stat[:, so_ + 12 + c0g:so_ + 14 + c0g]: s_.activation(
                    d, z, AF.Ln, bias=cvap("eps_ln"), scale=cvap("inv_dm")),
                waits=[(ch_s, sq_last)], inc=ch_s)
            v_rst = prog.op(
                "scalar",
                lambda s_, d=stat[:, so_ + 16 + c0g:so_ + 18 + c0g]:
                    s_.activation(d, d, AF.Exp, scale=cvap("neghalf")),
                waits=[(ch_s, v_lnv)], inc=ch_s)
            for c in grp:
                y1vals[c] = prog.op(
                    "vector",
                    lambda v, o=y1b[:, c * DM:(c + 1) * DM],
                    z=work[wl][:, c * DM:(c + 1) * DM],
                    nm=stat[:, so_ + 8 + c:so_ + 9 + c],
                    rs=stat[:, so_ + 16 + c:so_ + 17 + c]:
                        v.tensor_scalar(o, z, nm, rs, OP.add, OP.mult),
                    waits=[(ch_s, v_rst), (ch_yt, g["xbm_yt"])],
                    inc=ch_v)
                g_stat[wl] = ch_v.n
        if a_next_gen is not None:
            next(a_next_gen, None)
            next(a_next_gen, None)

        def y1t_tr(c):
            dview = y1T.ap().rearrange("p (j t) -> p j t", j=NJ)[
                :, :, c * 128:(c + 1) * 128]
            sview2 = y1b[:, c * DM:(c + 1) * DM].rearrange(
                "p (j f) -> p j f", j=NJ)
            prog.op("sync",
                    lambda s_, d=dview, z=sview2: s_.dma_start(
                        out=d, in_=z, transpose=True),
                    waits=[(ch_v, y1vals[c]), (ch_m, g["y1T"])],
                    inc=ch_yt, inc_n=16)

        for ci in range(2):
            stream_chunk2(sg, "wg", wg_buf, wg_s, wg_in, wg_slot_g,
                          wg_cval, ci)
            stream_chunk2(sg, "wu", wu_buf, wu_s, wu_in, wu_slot_g,
                          wu_cval, ci)
        for c in (2, 3, 0, 1):
            y1t_tr(c)
        stream_chunk2(sg, "wg", wg_buf, wg_s, wg_in, wg_slot_g, wg_cval, 2)
        stream_chunk2(sg, "wu", wu_buf, wu_s, wu_in, wu_slot_g, wu_cval, 2)
        yt_val = ch_yt.n
        g["xbm_yt"] = ch_yt.n
        for mp in range(NMI_F // 2):
            slot = mp % 3
            mi0 = 2 * mp
            m_gu = None
            for which, wbuf_, cvd, ps in ((0, wg_buf, wg_cval, C_ps),
                                          (1, wu_buf, wu_cval, D_ps)):
                pkey = "C" if which == 0 else "D"
                for sub in range(2):
                    woff = slot * 2048 + sub * 1024
                    for ki in range(NJ):
                        lhs = wbuf_[:, woff + ki * 128: woff + (ki + 1) * 128]
                        rhs = y1T[:, ki * ST:(ki + 1) * ST]
                        w = [cvd[mp], (ch_yt, yt_val)]
                        if ki == 0 and sub == 0:
                            w = w + [(chof(t), v) for t, v in gPS[pkey]]
                        fin = (ki == NJ - 1)
                        m_gu = prog.op(
                            "tensor",
                            lambda t, o=ps[:, sub * 512:(sub + 1) * 512],
                            a=lhs, b=rhs, f1=(ki == 0), f2=fin:
                                t.matmul(o, a, b, start=f1, stop=f2),
                            waits=w, inc=ch_m if fin else None, inc_n=NJ)
            wg_slot_g[slot] = ch_m.n
            wu_slot_g[slot] = ch_m.n
            if mp + 3 < 12:
                stream_chunk2(sg, "wg", wg_buf, wg_s, wg_in, wg_slot_g,
                              wg_cval, mp + 3)
                stream_chunk2(sg, "wu", wu_buf, wu_s, wu_in, wu_slot_g,
                              wu_cval, mp + 3)
            v_sg = prog.op("scalar",
                           lambda s_, d=sgb[mp % 2]: s_.activation(
                               d[:], C_ps[:], AF.Silu,
                               scale=cvap("inv127")),
                           waits=[(ch_m, m_gu), (ch_v, g_sg[mp % 2])],
                           inc=ch_s)
            v_h = prog.op("vector",
                          lambda v, o=hbuf[:, mi0 * ST:(mi0 + 2) * ST],
                          sg_=sgb[mp % 2]: v.tensor_tensor(
                              o, D_ps[:], sg_[:], OP.mult),
                          waits=[(ch_s, v_sg), (ch_m, g["h"])], inc=ch_v)
            g_sg[mp % 2] = v_h
            h_vals[mp] = v_h
            gPS["C"] = [("s", v_sg)]
            gPS["D"] = [("v", v_h)]
            if mp in (1, 3, 5, 7, 9) and a_next_gen is not None:
                next(a_next_gen, None)
            if mp in (2, 6):
                pop_den()
        g["y1T"] = ch_m.n
        pop_den()
        if a_next_gen is not None:
            next(a_next_gen, None)

        r2_half = [[None] * 2 for _ in range(NCH)]
        for h2 in range(2):
            pspair = (C_ps, D_ps) if h2 == 0 else (A_ps, B_ps)
            for k in range(min(6, NMI_F)):
                stream_half(sg, "wd", wd_buf, wd_s, wd_in, wd_slot_g,
                            wd_cval, k, h2, k)
            for k in range(NMI_F):
                ridx = k % 6
                for c in range(NCH):
                    ps = pspair[c // 2]
                    half = (c % 2)
                    w = [wd_cval[(k, h2)]]
                    if c == 0:
                        w = w + [(ch_v, h_vals[k // 2])]
                    if k == 0 and c == 0:
                        keys = ("C", "D") if h2 == 0 else ("A", "B")
                        w = w + [
                            (chof(t), v) for key in keys for t, v in gPS[key]]
                    prog.op(
                        "tensor",
                        lambda t, o=ps[:, half * 512:(half + 1) * 512],
                        a=hbuf[:, k * ST + c * 128: k * ST + (c + 1) * 128],
                        b=wd_buf[:, ridx * 512:(ridx + 1) * 512],
                        f1=(k == 0), f2=(k == NMI_F - 1):
                            t.matmul(o, a, b, start=f1, stop=f2),
                        waits=w, inc=ch_m if c == NCH - 1 else None,
                        inc_n=NCH)
                wd_slot_g[ridx] = ch_m.n
                if k + 6 < NMI_F:
                    stream_half(sg, "wd", wd_buf, wd_s, wd_in, wd_slot_g,
                                wd_cval, k + 6, h2, k + 6)
            m_wave = ch_m.n
            for c in range(NCH):
                ps = pspair[c // 2]
                half = (c % 2)
                v_r2 = prog.op(
                    "vector",
                    lambda v, o=work[wl][:, c * DM + h2 * 512:
                                         c * DM + h2 * 512 + 512],
                    p_=ps[:, half * 512:(half + 1) * 512],
                    yy=y1b[:, c * DM + h2 * 512: c * DM + h2 * 512 + 512],
                    a_=stat[:, so_ + c * 2 + h2: so_ + c * 2 + h2 + 1]:
                        v.scalar_tensor_tensor(
                            o, p_, S127_2, yy, OP.mult, OP.add, accum_out=a_),
                    waits=[(ch_m, m_wave), (ch_v, g_stat[wl])], inc=ch_v)
                r2_half[c][h2] = v_r2
            if h2 == 0:
                gPS["C"] = [("v", ch_v.n)]
                gPS["D"] = [("v", ch_v.n)]
                pop_den()
                if a_next_gen is not None:
                    next(a_next_gen, None)
            else:
                gPS["A"] = [("v", ch_v.n)]
                gPS["B"] = [("v", ch_v.n)]
        g["h"] = ch_m.n

        sview = stat[:, so_:so_ + 8].rearrange("p (c u) -> p c u", u=2)
        v_s1b = prog.op("vector",
                        lambda v, d=stat[:, so_ + 8:so_ + 12],
                        a=sview[:, :, 0], b=sview[:, :, 1]:
                            v.tensor_tensor(d, a, b, OP.add),
                        waits=[(ch_v, ch_v.n)], inc=ch_v)
        v_nmb = prog.op("vector",
                        lambda v, d=stat[:, so_ + 8:so_ + 12]:
                            v.tensor_scalar(d, d, -1.0 / DM, None, OP.mult),
                        waits=[(ch_v, v_s1b)], inc=ch_v)
        sq_vals = []
        for c in range(NCH):
            v_sq = prog.op("scalar",
                           lambda s_, z=work[wl][:, c * DM:(c + 1) * DM],
                           b=stat[:, so_ + 8 + c:so_ + 9 + c],
                           a_=stat[:, so_ + 12 + c:so_ + 13 + c]:
                               s_.activation(ebuf[:], z, AF.Square, bias=b,
                                             accum_out=a_),
                           waits=[(ch_v, v_nmb), (ch_v, g["eb_r"]),
                                  (ch_s, g["eb_s"])],
                           inc=ch_s)
            g["eb_s"] = v_sq
            sq_vals.append(v_sq)
        v_lnv = prog.op("scalar",
                        lambda s_, d=stat[:, so_ + 16:so_ + 20],
                        z=stat[:, so_ + 12:so_ + 16]: s_.activation(
                            d, z, AF.Ln, bias=cvap("eps_ln"),
                            scale=cvap("inv_dm")),
                        waits=[(ch_s, sq_vals[-1])], inc=ch_s)
        v_rst = prog.op("scalar",
                        lambda s_, d=stat[:, so_ + 16:so_ + 20]:
                            s_.activation(d, d, AF.Exp,
                                          scale=cvap("neghalf")),
                        waits=[(ch_s, v_lnv)], inc=ch_s)
        for c in range(NCH):
            v_o = prog.op("vector",
                          lambda v, o=work[wl][:, c * DM:(c + 1) * DM],
                          nm=stat[:, so_ + 8 + c:so_ + 9 + c],
                          rs=stat[:, so_ + 16 + c:so_ + 17 + c]:
                              v.tensor_scalar(o, o, nm, rs, OP.add,
                                              OP.mult),
                          waits=[(ch_s, v_rst)], inc=ch_v)
            g_stat[wl] = ch_v.n
        v_out_ready = ch_v.n

        for c in range(NCH):
            for hf in range(2):
                odst = out_d[s, st * ST + c * 128: st * ST + (c + 1) * 128,
                             hf * 512:(hf + 1) * 512]
                zsl = work[wl][:, c * DM + hf * 512: c * DM + (hf + 1) * 512]
                prog.op("scalar",
                        lambda s_, d=odst, z=zsl: s_.dma_start(out=d, in_=z),
                        waits=[(ch_v, v_out_ready)], inc=ch_od2[wl],
                        inc_n=16)
                n_out_dma[0] += 1
        g_work[wl] = ch_od2[wl].n

    emit_conversions()
    emit_xcast(0)
    a_gen = emit_A(0)
    _SENT = object()
    nseg = 0
    while next(a_gen, _SENT) is not _SENT:
        nseg += 1
        if nseg % 2 == 0:
            pop_den()
    while den_q:
        pop_den()
    for sg in range(N_SUBTILES):
        nxt = emit_A(sg + 1) if sg + 1 < N_SUBTILES else None
        emit_BCD(sg, nxt)
        run_gen(nxt)

    prog.op("scalar", lambda s_: s_.nop(),
            waits=[(c, c.n) for c in ch_od2 if c.n])

    block = e(nc.Block())
    prog.replay(block)
    ctx.close()
    return nc


def _prep_weights(w_qkv, w_out, w_gate, w_up, w_down):
    order = []
    for pr in range(4):
        order += [8 + 2 * pr, 9 + 2 * pr, 16 + 2 * pr, 17 + 2 * pr,
                  0 + 2 * pr, 1 + 2 * pr]
    wq = np.empty((3 * NJ, 128, NJ * 128), dtype=np.float32)
    for i, t in enumerate(order):
        wq[i] = (w_qkv[:, t * 128:(t + 1) * 128]
                 .reshape(NJ, 128, 128).transpose(1, 0, 2).reshape(128, -1))
    wo = np.ascontiguousarray(w_out.reshape(NKT, 128, DM), dtype=np.float32)
    wgT = np.ascontiguousarray(w_gate.T)
    wuT = np.ascontiguousarray(w_up.T)
    wdT = np.ascontiguousarray(w_down.T)

    def tile_kmaj(wt):
        a = np.empty((NMI_F, 128, NJ * 128), dtype=np.int32)
        for mi in range(NMI_F):
            a[mi] = (wt[:, mi * 128:(mi + 1) * 128]
                     .reshape(NJ, 128, 128).transpose(1, 0, 2)
                     .reshape(128, -1))
        return a

    wg = tile_kmaj(wgT)
    wu = tile_kmaj(wuT)
    wd = np.ascontiguousarray(wdT.reshape(NMI_F, 128, DM))
    return wq, wo, wg, wu, wd


def make_inputs(L0_attn_x, L0_attn_w_qkv, L0_attn_w_out, L0_attn_alpha,
                L0_attn_beta, L0_ffn_w_gate, L0_ffn_w_up, L0_ffn_w_down,
                ncore=NCORE):
    x = np.ascontiguousarray(np.asarray(L0_attn_x, dtype=np.float32))
    wq, wo, wg, wu, wd = _prep_weights(
        np.asarray(L0_attn_w_qkv, dtype=np.float32),
        np.asarray(L0_attn_w_out, dtype=np.float32),
        np.asarray(L0_ffn_w_gate, dtype=np.int32),
        np.asarray(L0_ffn_w_up, dtype=np.int32),
        np.asarray(L0_ffn_w_down, dtype=np.int32))
    ab = np.concatenate([np.asarray(L0_attn_alpha, dtype=np.float32),
                         np.asarray(L0_attn_beta, dtype=np.float32)])[None, :]
    cv = np.zeros((128, 8), dtype=np.float32)
    for name, (idx, val) in CV.items():
        cv[:, idx] = val
    bones = np.zeros((128, 128), dtype=np.float32)
    bones[0:64, 0:64] = 1.0
    bones[64:128, 64:128] = 1.0
    bones = bones.astype(ml_dtypes.bfloat16)
    ones1 = np.ones((1, 128), dtype=np.float32)
    in_maps = []
    for c in range(ncore):
        in_maps.append({
            "x": np.ascontiguousarray(x[2 * c:2 * c + 2]),
            "wq": wq, "wo": wo, "wg": wg, "wu": wu, "wd": wd,
            "ab": ab, "cv": cv, "bones": bones, "ones1": ones1,
        })
    return in_maps


def kernel(L0_attn_x, L0_attn_w_qkv, L0_attn_w_out, L0_attn_alpha,
           L0_attn_beta, L0_ffn_w_gate, L0_ffn_w_up, L0_ffn_w_down):
    in_maps = make_inputs(L0_attn_x, L0_attn_w_qkv, L0_attn_w_out,
                          L0_attn_alpha, L0_attn_beta, L0_ffn_w_gate,
                          L0_ffn_w_up, L0_ffn_w_down)
    nc = build_nc()
    res = run_bass_kernel_spmd(nc, in_maps, core_ids=list(range(NCORE)))
    out = np.concatenate(
        [np.asarray(r["out"], dtype=np.float32) for r in res.results], axis=0)
    return out.reshape(16, T, DM)


# version 25
# speedup vs baseline: 1.0162x; 1.0162x over previous
import numpy as np
import ml_dtypes
from contextlib import ExitStack

import concourse.bass as bass
import concourse.mybir as mybir
from concourse.bass_utils import run_bass_kernel_spmd

F32 = mybir.dt.float32
BF16 = mybir.dt.bfloat16
I32 = mybir.dt.int32
AF = mybir.ActivationFunctionType
OP = mybir.AluOpType

NCORE = 8
SEQ = 2
T = 2048
DM = 1024
NH = 16
ST = 512
NST = T // ST
NCH = ST // 128
NJ = DM // 128
NMI_F = 3072 // 128
NKT = 2 * NJ
S127_2 = float(np.float32(1.0) / (np.float32(127.0) * np.float32(127.0)))

N_SUBTILES = SEQ * NST
N_SRC_STREAM = 0

CV = {
    "neg1": (0, -1.0),
    "neghalf": (1, -0.5),
    "eps_ln": (2, 1e-5),
    "inv_dm": (3, 1.0 / 1024.0),
    "inv127": (4, float(np.float32(1.0) / np.float32(127.0))),
    "neg_eps8": (5, -1e-8),
}

ENGINES = ("sync", "vector", "scalar", "gpsimd", "tensor")


class Ch:
    def __init__(self, sem, name):
        self.sem = sem
        self.name = name
        self.n = 0


class Prog:
    def __init__(self):
        self.items = {e: [] for e in ENGINES}

    def op(self, eng, fn, waits=(), inc=None, inc_n=1):
        w = [(ch, v) for ch, v in waits if v > 0]
        post = None
        if inc is not None:
            inc.n += inc_n
            post = (inc, inc_n)
        self.items[eng].append((fn, w, post))
        return inc.n if inc is not None else None

    def replay(self, block):
        for eng_name in ENGINES:
            items = self.items[eng_name]

            def make(eng_items):
                def body(eng):
                    seen = {}
                    for fn, waits, post in eng_items:
                        for ch, v in waits:
                            if seen.get(ch, 0) >= v:
                                continue
                            eng.wait_ge(ch.sem, v)
                            seen[ch] = v
                        ins = fn(eng)
                        if post is not None:
                            ch, k = post
                            ins.then_inc(ch.sem, k)
                return body

            getattr(block, eng_name)(make(items))


def build_nc():
    nc = bass.Bass("TRN2", target_bir_lowering=False)

    x_in = nc.dram_tensor("x", [SEQ, T, DM], F32, kind="ExternalInput").ap()
    wq_in = nc.dram_tensor("wq", [3 * NJ, 128, NJ * 128], F32,
                           kind="ExternalInput").ap()
    wo_in = nc.dram_tensor("wo", [NKT, 128, DM], F32, kind="ExternalInput").ap()
    wg_in = nc.dram_tensor("wg", [NMI_F, 128, NJ * 128], I32,
                           kind="ExternalInput").ap()
    wu_in = nc.dram_tensor("wu", [NMI_F, 128, NJ * 128], I32,
                           kind="ExternalInput").ap()
    wd_in = nc.dram_tensor("wd", [NMI_F, 128, DM], I32,
                           kind="ExternalInput").ap()
    ab_in = nc.dram_tensor("ab", [1, 2 * NH], F32, kind="ExternalInput").ap()
    cv_in = nc.dram_tensor("cv", [128, 8], F32, kind="ExternalInput").ap()
    bones_in = nc.dram_tensor("bones", [128, 128], BF16,
                              kind="ExternalInput").ap()
    ones1_in = nc.dram_tensor("ones1", [1, 128], F32, kind="ExternalInput").ap()
    out_d = nc.dram_tensor("out", [SEQ, T, DM], F32, kind="ExternalOutput").ap()

    wq_s = nc.dram_tensor("wq_s", [3 * NJ, 128, NJ * 128], BF16,
                          kind="Internal").ap()
    wg_s = nc.dram_tensor("wg_s", [NMI_F, 128, NJ * 128], BF16,
                          kind="Internal").ap()
    wu_s = nc.dram_tensor("wu_s", [NMI_F, 128, NJ * 128], BF16,
                          kind="Internal").ap()
    wo_s = nc.dram_tensor("wo_s", [NKT, 128, DM], BF16, kind="Internal").ap()
    wd_s = nc.dram_tensor("wd_s", [NMI_F, 128, DM], BF16, kind="Internal").ap()
    xbf_d = nc.dram_tensor("xbf", [SEQ, T, DM], BF16, kind="Internal").ap()

    ctx = ExitStack()
    e = ctx.enter_context

    cv_sb = e(nc.sbuf_tensor("cv_sb", [128, 8], F32))
    bones_sb = e(nc.sbuf_tensor("bones_sb", [128, 128], BF16))
    ones1_sb = e(nc.sbuf_tensor("ones1_sb", [1, 128], F32))
    ab_sb = e(nc.sbuf_tensor("ab_sb", [1, 2 * NH], F32))
    dtmp_sb = e(nc.sbuf_tensor("dtmp_sb", [1, 2 * NH], F32))
    dbc_sb = e(nc.sbuf_tensor("dbc_sb", [128, 2 * NH], F32))
    dcols_sb = e(nc.sbuf_tensor("dcols_sb", [128, 16], F32))
    states_sb = e(nc.sbuf_tensor("states_sb", [128, 32], F32))

    wq_buf = e(nc.sbuf_tensor("wq_buf", [128, 3 * 2048], BF16))
    wg_buf = e(nc.sbuf_tensor("wg_buf", [128, 3 * 2048], BF16))
    wu_buf = e(nc.sbuf_tensor("wu_buf", [128, 3 * 2048], BF16))
    wo_buf = e(nc.sbuf_tensor("wo_buf", [128, 6 * 512], BF16))
    wd_buf = e(nc.sbuf_tensor("wd_buf", [128, 6 * 512], BF16))


    xbm = e(nc.sbuf_tensor("xbm", [128, NCH * DM], BF16))
    xT = e(nc.sbuf_tensor("xT", [128, NJ * ST], BF16))
    work = [e(nc.sbuf_tensor(f"work{i}", [128, NCH * DM], F32))
            for i in range(2)]
    ebuf = e(nc.sbuf_tensor("ebuf", [128, 1024], F32))
    kphi = e(nc.sbuf_tensor("kphi", [128, 1024], F32))
    kvb = e(nc.sbuf_tensor("kvb", [128, 1024], F32))
    qb = [e(nc.sbuf_tensor(f"qb{i}", [128, 1024], F32)) for i in range(2)]
    runs = e(nc.sbuf_tensor("runs", [128, 8 * ST], F32))
    denr = e(nc.sbuf_tensor("denr", [128, 1024], BF16))
    denc = e(nc.sbuf_tensor("denc", [128, 1024], F32))
    concat = e(nc.sbuf_tensor("concat", [128, NKT * ST], BF16))
    y1T = e(nc.sbuf_tensor("y1T", [128, NJ * ST], BF16))
    y1b = e(nc.sbuf_tensor("y1b", [128, NCH * DM], BF16))
    sgb = [e(nc.sbuf_tensor(f"sgb{i}", [128, 2 * ST], BF16)) for i in range(2)]
    hbuf = e(nc.sbuf_tensor("hbuf", [128, NMI_F * ST], BF16))
    stat = e(nc.sbuf_tensor("stat", [128, 48], F32))

    for name, (idx, val) in CV.items():
        nc.const_aps.register(F32, val, cv_sb[:, idx:idx + 1])

    A_ps = e(nc.psum_tensor("A_ps", [128, 1024], F32))
    B_ps = e(nc.psum_tensor("B_ps", [128, 1024], F32))
    C_ps = e(nc.psum_tensor("C_ps", [128, 1024], F32))
    D_ps = e(nc.psum_tensor("D_ps", [128, 1024], F32))

    def mkch(name):
        return Ch(e(nc.semaphore(name)), name)

    ch_v = mkch("ch_v")
    ch_s = mkch("ch_s")
    ch_p = mkch("ch_p")
    ch_m = mkch("ch_m")
    ch_ld = mkch("ch_ld")
    ch_xc = mkch("ch_xc")
    ch_xt = mkch("ch_xt")
    ch_yt = mkch("ch_yt")
    ch_w = {key: [mkch(f"ch_w_{key}{i}") for i in range(6 if key in ("wo", "wd") else 3)]
            for key in ("wq", "wg", "wu", "wo", "wd")}
    ch_wq_g = [mkch(f"ch_wq_g{i}") for i in range(3)]
    ch_od2 = [mkch("ch_od0"), mkch("ch_od1")]

    prog = Prog()

    def cvap(name, parts=128):
        return cv_sb[0:parts, CV[name][0]:CV[name][0] + 1]

    def ld(dst, src):
        prog.op("sync", lambda s, d=dst, z=src: s.dma_start(out=d, in_=z),
                inc=ch_ld, inc_n=16)

    ld(cv_sb[:], cv_in)
    ld(bones_sb[:], bones_in)
    ld(ones1_sb[:], ones1_in)
    ld(ab_sb[:], ab_in)
    lds_done = ch_ld.n

    v_sig = prog.op("scalar",
                    lambda s: s.activation(dtmp_sb[:], ab_sb[:], AF.Sigmoid),
                    waits=[(ch_ld, lds_done)], inc=ch_s)
    v_rel = prog.op("scalar",
                    lambda s: s.activation(dtmp_sb[:], dtmp_sb[:], AF.Relu,
                                           bias=cvap("neg_eps8", 1)),
                    waits=[(ch_s, v_sig)], inc=ch_s)
    v_add = prog.op("vector",
                    lambda v: v.tensor_scalar(dtmp_sb[:], dtmp_sb[:], 1e-8,
                                              None, OP.add),
                    waits=[(ch_s, v_rel)], inc=ch_v)
    v_bc = prog.op("tensor",
                   lambda t: t.matmul(A_ps[:, 0:2 * NH], ones1_sb[:],
                                      dtmp_sb[:], start=True, stop=True),
                   waits=[(ch_v, v_add), (ch_ld, lds_done)], inc=ch_m)
    v_evac = prog.op("vector",
                     lambda v: v.tensor_copy(dbc_sb[:], A_ps[:, 0:2 * NH]),
                     waits=[(ch_m, v_bc)], inc=ch_v)
    for j in range(NJ):
        for dec in range(2):
            src0 = dbc_sb[0:64, dec * NH + 2 * j: dec * NH + 2 * j + 1]
            src1 = dbc_sb[64:128, dec * NH + 2 * j + 1: dec * NH + 2 * j + 2]
            dst0 = dcols_sb[0:64, dec * 8 + j: dec * 8 + j + 1]
            dst1 = dcols_sb[64:128, dec * 8 + j: dec * 8 + j + 1]
            prog.op("vector", lambda v, a=dst0, b=src0: v.tensor_copy(a, b),
                    waits=[(ch_v, v_evac)], inc=ch_v)
            prog.op("vector", lambda v, a=dst1, b=src1: v.tensor_copy(a, b),
                    waits=[(ch_v, v_evac)], inc=ch_v)
    dcols_ready = ch_v.n

    conv_ready = {"wq": [None] * (3 * NJ), "wo": [None] * NKT,
                  "wg": [None] * NMI_F, "wu": [None] * NMI_F,
                  "wd": [None] * NMI_F, "x": [None] * N_SUBTILES}

    def chof(tag):
        return {"v": ch_v, "s": ch_s, "p": ch_p, "m": ch_m}[tag]

    def emit_conversions():
        def xconv(sg, gated):
            st_, stt = divmod(sg, NST)
            ch = mkch(f"cc_x_{sg}")
            for c in range(NCH):
                r0 = stt * ST + c * 128
                prog.op("gpsimd",
                        lambda g, d=xbf_d[st_, r0:r0 + 128, :],
                        z=x_in[st_, r0:r0 + 128, :]: g.dma_start(out=d, in_=z),
                        waits=([(ch_xt, 4 * 16)] if gated else []),
                        inc=ch, inc_n=16)
            conv_ready["x"][sg] = (ch, NCH * 16)

        xconv(0, False)
        for c in range(NCH):
            prog.op("gpsimd",
                    lambda g, d=xbm[:, c * DM:(c + 1) * DM],
                    z=xbf_d[0, c * 128:(c + 1) * 128, :]:
                        g.dma_start(out=d, in_=z),
                    waits=[conv_ready["x"][0]], inc=ch_xc, inc_n=16)
        xcast_done[0] = ch_xc.n

        srcs = {"wq": (wq_in, wq_s), "wo": (wo_in, wo_s),
                "wg": (wg_in, wg_s), "wu": (wu_in, wu_s),
                "wd": (wd_in, wd_s)}

        def conv_chunk(key, c0, csz=4):
            src, dst = srcs[key]
            c1 = c0 + csz
            ch = mkch(f"cc_{key}_{c0}")
            for mi in range(c0, c1):
                prog.op("gpsimd",
                        lambda g, d=dst[mi], z=src[mi]:
                            g.dma_start(out=d, in_=z),
                        waits=[(ch_xt, 4 * 16)], inc=ch, inc_n=16)
            for mi in range(c0, c1):
                conv_ready[key][mi] = (ch, (c1 - c0) * 16)

        order = [("wq", 0), ("wq", 4), ("wq", 8), ("wq", 12), ("x", 1),
                 ("wq", 16), ("wq", 20), ("wo", 0), ("wo", 4), ("x", 2),
                 ("wo", 8), ("wo", 12), ("wg", 0), ("wu", 0),
                 ("wg", 4), ("wu", 4), ("x", 3), ("wg", 8), ("wu", 8),
                 ("wd", 0), ("wd", 4), ("x", 4), ("wg", 12), ("wu", 12),
                 ("wd", 8), ("x", 5), ("wg", 16), ("wu", 16),
                 ("wd", 12), ("x", 6), ("wg", 20), ("wu", 20),
                 ("wd", 16), ("wd", 20), ("x", 7)]
        for key, c0 in order:
            if key == "x":
                if c0 < N_SUBTILES:
                    xconv(c0, True)
            else:
                conv_chunk(key, c0)

    wq_cval = {}
    wg_cval = {}
    wu_cval = {}
    wo_cval = {}
    wd_cval = {}
    wq_slot_g = [0, 0, 0]
    wg_slot_g = [0, 0, 0]
    wu_slot_g = [0, 0, 0]
    wo_slot_g = [0] * 6
    wd_slot_g = [0] * 6

    def stream_chunk2(sg, key, buf, scratch, src4, slot_g, cval, ci,
                      q="sync"):
        sl = ci % 3
        wch = ch_wq_g[sl] if (key == "wq" and q == "gpsimd") else ch_w[key][sl]
        mi0 = 2 * ci
        rdy = [conv_ready[key][mi0], conv_ready[key][mi0 + 1]]
        assert all(r is not None for r in rdy), (key, mi0)
        for a in range(2):
            dst2 = buf[:, sl * 2048 + a * 1024: sl * 2048 + (a + 1) * 1024]
            prog.op(q,
                    lambda s, d=dst2, z=scratch[mi0 + a]:
                        s.dma_start(out=d, in_=z),
                    waits=rdy + [(ch_m, slot_g[sl])],
                    inc=wch, inc_n=16)
        cval[ci] = (wch, wch.n)

    def stream_half(sg, key, buf, scratch, src3, slot_g, cval, k, h2, ridx):
        sl = ridx % 6
        wch = ch_w[key][sl]
        dst = buf[:, sl * 512:(sl + 1) * 512]
        if sg < N_SRC_STREAM:
            src = src3[k][:, h2 * 512:(h2 + 1) * 512]
            prog.op("gpsimd",
                    lambda g, d=dst, z=src: g.dma_start(out=d, in_=z),
                    waits=[(ch_m, slot_g[sl])], inc=wch, inc_n=16)
        else:
            src = scratch[k][:, h2 * 512:(h2 + 1) * 512]
            rdy = conv_ready[key][k]
            assert rdy is not None, (key, k)
            prog.op("sync",
                    lambda s, d=dst, z=src: s.dma_start(out=d, in_=z),
                    waits=[rdy, (ch_m, slot_g[sl])],
                    inc=wch, inc_n=16)
        cval[(k, h2)] = (wch, wch.n)

    g = dict(xbm_v=0, xbm_yt=0, xT=0, y1T=0, concat=0, h=0, runs=0,
             kphi=0, kvb=0, eb_r=0, denr=0, denc=0, stat_v=0, stat_s=0,
             runs_p=0, concat_ready=0, eb_s=0)
    g_stat = [0, 0]
    g_q = [0, 0]
    g_sg = [0, 0]
    g_work = [0, 0]
    gPS = {"A": [("v", v_evac)], "B": [("v", 0)],
           "C": [("v", 0)], "D": [("v", 0)]}

    n_out_dma = [0]
    xcast_done = {}

    def emit_xcast(sg):
        if sg in xcast_done:
            return xcast_done[sg]
        s, st = divmod(sg, NST)
        for c in range(NCH):
            xsrc = xbf_d[s, st * ST + c * 128: st * ST + (c + 1) * 128, :]
            prog.op("gpsimd",
                    lambda gg, z=xsrc, d=xbm[:, c * DM:(c + 1) * DM]:
                        gg.dma_start(out=d, in_=z),
                    waits=[(ch_v, g["xbm_v"]), conv_ready["x"][sg]],
                    inc=ch_xc, inc_n=16)
        xcast_done[sg] = ch_xc.n
        return ch_xc.n

    def emit_den(pr, qp, v_q, v_scans, num_vals):
        for jj in range(2):
            j = 2 * pr + jj
            qv = qb[qp][:, jj * 512:(jj + 1) * 512].rearrange(
                "p (o t) -> p o t", o=1).broadcast_to([128, 2, 512])
            kvpart = runs[:, jj * 2048: jj * 2048 + 1024]
            kspart = runs[:, jj * 2048 + 1024: jj * 2048 + 2048]
            v_dr = prog.op(
                "gpsimd",
                lambda v, q_=qv, r_=kspart: v.tensor_tensor(
                    denr.ap().rearrange("p (o t) -> p o t", o=2), q_,
                    r_.rearrange("p (o t) -> p o t", o=2), OP.mult),
                waits=[(ch_v, v_scans), (ch_v, v_q),
                       (ch_m, g["denr"])], inc=ch_p)
            wB2 = [(chof(t), v) for t, v in gPS["B"]]
            m_d = None
            for dec in range(2):
                m_d = prog.op(
                    "tensor",
                    lambda t, o=B_ps[:, dec * 512:(dec + 1) * 512],
                    r_=denr[:, dec * 512:(dec + 1) * 512]:
                        t.matmul(o, bones_sb[:], r_, start=True,
                                 stop=True),
                    waits=([(ch_p, v_dr)] + (wB2 if dec == 0 else [])),
                    inc=ch_m if dec == 1 else None, inc_n=2)
            g["denr"] = ch_m.n
            v_cl = prog.op("vector",
                           lambda v: v.tensor_scalar(
                               denc[:], B_ps[:], 1e-6, None, OP.max),
                           waits=[(ch_m, m_d), (ch_v, g["denc"])],
                           inc=ch_v)
            gPS["B"] = [("v", v_cl)]
            v_ln = prog.op("scalar",
                           lambda s_: s_.activation(denc[:], denc[:],
                                                    AF.Ln),
                           waits=[(ch_v, v_cl)], inc=ch_s)
            v_ex = prog.op("scalar",
                           lambda s_: s_.activation(
                               denc[:], denc[:], AF.Exp,
                               scale=cvap("neg1")),
                           waits=[(ch_s, v_ln)], inc=ch_s)
            num_val = num_vals[jj]
            cview = concat.ap().rearrange(
                "p (d e t) -> p d e t", d=2, e=NJ)[:, :, j, :]
            v_of = prog.op(
                "vector",
                lambda v, o=cview, r_=kvpart: v.tensor_tensor(
                    o, r_.rearrange("p (o t) -> p o t", o=2),
                    denc.ap().rearrange("p (o t) -> p o t", o=2),
                    OP.mult),
                waits=[(ch_s, v_ex), (ch_m, g["concat"]),
                       (ch_p, num_val)],
                inc=ch_v)
            g["denc"] = v_of
            g["concat_ready"] = v_of
        g["runs"] = ch_v.n
        g_q[qp] = ch_v.n

    den_q = []

    def pop_den():
        if den_q:
            emit_den(*den_q.pop(0))

    def emit_A(sg):
        s, st = divmod(sg, NST)
        wq_q = "gpsimd" if sg >= 3 else "scalar"
        for ci in range(3):
            stream_chunk2(sg, "wq", wq_buf, wq_s, wq_in, wq_slot_g,
                          wq_cval, ci, q=wq_q)
        emit_xcast(sg)
        for c in range(NCH):
            dview = xT.ap().rearrange("p (j t) -> p j t", j=NJ)[
                :, :, c * 128:(c + 1) * 128]
            sview = xbf_d[s, st * ST + c * 128: st * ST + (c + 1) * 128,
                          :].rearrange("p (j f) -> p j f", j=NJ)
            prog.op("sync",
                    lambda s_, d=dview, z=sview: s_.dma_start(
                        out=d, in_=z, transpose=True),
                    waits=[conv_ready["x"][sg], (ch_m, g["xT"])],
                    inc=ch_xt, inc_n=16)
        xt_val = ch_xt.n

        def qkv_mm(mi, ps, col_off, extra_waits):
            ci = mi // 2
            slot = ci % 3
            woff = slot * 2048 + (mi % 2) * 1024
            last = None
            for ki in range(NJ):
                lhs = wq_buf[:, woff + ki * 128: woff + (ki + 1) * 128]
                rhs = xT[:, ki * ST:(ki + 1) * ST]
                w = [wq_cval[ci], (ch_xt, xt_val)]
                if ki == 0:
                    w = w + extra_waits
                fin = (ki == NJ - 1)
                last = prog.op(
                    "tensor",
                    lambda t, o=ps[:, col_off:col_off + 512], a=lhs, b=rhs,
                    f1=(ki == 0), f2=fin:
                        t.matmul(o, a, b, start=f1, stop=f2),
                    waits=w, inc=ch_m if fin else None, inc_n=NJ)
            wq_slot_g[slot] = ch_m.n
            if mi % 2 == 1 and ci + 3 < 12:
                stream_chunk2(sg, "wq", wq_buf, wq_s, wq_in, wq_slot_g,
                              wq_cval, ci + 3, q=wq_q)
            return last

        for pr in range(NJ // 2):
            qp = pr % 2
            base = pr * 6
            wA = [(chof(t), v) for t, v in gPS["A"]]
            qkv_mm(base + 0, A_ps, 0, wA)
            m_k = qkv_mm(base + 1, A_ps, 512, [])
            v_e = prog.op("scalar",
                          lambda s_: s_.activation(ebuf[:], A_ps[:], AF.Exp),
                          waits=[(ch_m, m_k), (ch_v, g["eb_r"]),
                                 (ch_s, g["eb_s"])], inc=ch_s)
            g["eb_s"] = v_e
            v_r = prog.op("scalar",
                          lambda s_: s_.activation(kphi[:], A_ps[:], AF.Relu),
                          waits=[(ch_m, m_k), (ch_v, g["kphi"])], inc=ch_s)
            v_kphi = prog.op("vector",
                             lambda v: v.scalar_tensor_tensor(
                                 kphi[:], ebuf[:], 1.0, kphi[:], OP.min,
                                 OP.add),
                             waits=[(ch_s, v_e), (ch_s, v_r)], inc=ch_v)
            gPS["A"] = [("s", v_e), ("s", v_r)]
            g["eb_r"] = v_kphi

            wB = [(chof(t), v) for t, v in gPS["B"]]
            qkv_mm(base + 2, B_ps, 0, wB)
            m_v = qkv_mm(base + 3, B_ps, 512, [])
            v_kv = prog.op("vector",
                           lambda v: v.tensor_tensor(kvb[:], kphi[:],
                                                     B_ps[:], OP.mult),
                           waits=[(ch_m, m_v), (ch_v, v_kphi),
                                  (ch_v, g["kvb"])], inc=ch_v)
            gPS["B"] = [("v", v_kv)]

            m_q0 = qkv_mm(base + 4, A_ps, 0,
                          [(ch_v, v_kphi), (ch_s, v_r), (ch_s, v_e)])
            m_q = qkv_mm(base + 5, A_ps, 512, [])
            v_eq = prog.op("scalar",
                           lambda s_: s_.activation(ebuf[:], A_ps[:], AF.Exp),
                           waits=[(ch_m, m_q), (ch_v, v_kphi),
                                  (ch_v, g["eb_r"]), (ch_s, g["eb_s"])],
                           inc=ch_s)
            g["eb_s"] = v_eq
            v_rq = prog.op("scalar",
                           lambda s_, d=qb[qp]: s_.activation(
                               d[:], A_ps[:], AF.Relu),
                           waits=[(ch_m, m_q), (ch_v, g_q[qp])], inc=ch_s)
            v_q = prog.op("vector",
                          lambda v, d=qb[qp]: v.scalar_tensor_tensor(
                              d[:], ebuf[:], 1.0, d[:], OP.min, OP.add),
                          waits=[(ch_s, v_eq), (ch_s, v_rq)], inc=ch_v)
            gPS["A"] = [("s", v_eq), ("s", v_rq)]
            g["eb_r"] = v_q

            yield

            for kind in (1, 0):
                for jj in range(2):
                    j = 2 * pr + jj
                    srcbuf = kvb if kind == 0 else kphi
                    src = srcbuf[:, jj * 512:(jj + 1) * 512]
                    src_val = v_kv if kind == 0 else v_kphi
                    for dec in range(2):
                        off = jj * 2048 + (kind * 2 + dec) * 512
                        sidx = j * 4 + kind * 2 + dec
                        dc = dcols_sb[:, dec * 8 + j: dec * 8 + j + 1]
                        init = 0.0 if st == 0 else states_sb[:, sidx:sidx + 1]
                        v_sc = prog.op(
                            "vector",
                            lambda v, o=runs[:, off:off + 512], d=dc, z=src,
                            ii=init: v.tensor_tensor_scan(
                                o, d.broadcast_to([128, 512]), z, ii,
                                OP.mult, OP.add),
                            waits=[(ch_v, src_val), (ch_v, dcols_ready),
                                   (ch_v, g["runs"])],
                            inc=ch_v)
            prog.op("vector",
                    lambda v, d=states_sb[:, 8 * pr: 8 * pr + 8],
                    z=runs.ap().rearrange("p (n t) -> p n t", n=8)[:, :, 511]:
                        v.tensor_copy(d, z),
                    waits=[(ch_v, ch_v.n)], inc=ch_v)
            v_scans = ch_v.n
            g["kphi"] = v_scans
            g["kvb"] = v_scans

            num_vals = []
            for jj in range(2):
                qv = qb[qp][:, jj * 512:(jj + 1) * 512].rearrange(
                    "p (o t) -> p o t", o=1).broadcast_to([128, 2, 512])
                kvpart = runs[:, jj * 2048: jj * 2048 + 1024]
                prog.op(
                    "gpsimd",
                    lambda v, q_=qv, r_=kvpart: v.tensor_tensor(
                        r_.rearrange("p (o t) -> p o t", o=2), q_,
                        r_.rearrange("p (o t) -> p o t", o=2), OP.mult),
                    waits=[(ch_v, v_scans), (ch_v, v_q)], inc=ch_p)
                num_vals.append(ch_p.n)
            den_q.append((pr, qp, v_q, v_scans, num_vals))
            yield
        g["xT"] = ch_m.n

    def run_gen(gen):
        if gen is None:
            return
        for _ in gen:
            pass

    def emit_BCD(sg, a_next_gen):
        s, st = divmod(sg, NST)
        wl = sg % 2
        so_ = wl * 24
        concat_val = g["concat_ready"]
        y1vals = {}
        h_vals = {}

        firstAB = [(ch_v, concat_val)] + [
            (chof(t), v) for key in ("A", "B") for t, v in gPS[key]]
        firstCD = [(chof(t), v) for key in ("C", "D") for t, v in gPS[key]]
        bank = (C_ps, D_ps, A_ps, B_ps)

        def lhs_wout(kt, c):
            return concat[:, kt * ST + c * 128: kt * ST + (c + 1) * 128]

        for k in range(3):
            for h2 in range(2):
                stream_half(sg, "wo", wo_buf, wo_s, wo_in, wo_slot_g,
                            wo_cval, k, h2, k * 2 + h2)
        for kt in range(NKT):
            for h2 in range(2):
                ridx = (kt * 2 + h2) % 6
                for ci, c in enumerate((2, 3, 0, 1)):
                    w = [wo_cval[(kt, h2)]]
                    if kt == 0 and h2 == 0:
                        if c == 2:
                            w = w + firstAB
                        elif c == 0:
                            w = w + firstCD
                    prog.op(
                        "tensor",
                        lambda t, o=bank[c][:, h2 * 512:(h2 + 1) * 512],
                        a=lhs_wout(kt, c),
                        b=wo_buf[:, ridx * 512:(ridx + 1) * 512],
                        f1=(kt == 0), f2=(kt == NKT - 1):
                            t.matmul(o, a, b, start=f1, stop=f2),
                        waits=w, inc=ch_m if ci == NCH - 1 else None,
                        inc_n=NCH)
                wo_slot_g[ridx] = ch_m.n
            if kt + 3 < NKT:
                for h2 in range(2):
                    stream_half(sg, "wo", wo_buf, wo_s, wo_in, wo_slot_g,
                                wo_cval, kt + 3, h2, (kt + 3) * 2 + h2)
        m_wave = ch_m.n
        gb = {}
        for c in (2, 3, 0, 1):
            for h2 in range(2):
                prog.op(
                    "vector",
                    lambda v, o=work[wl][:, c * DM + h2 * 512:
                                         c * DM + h2 * 512 + 512],
                    p_=bank[c][:, h2 * 512:(h2 + 1) * 512],
                    xx=xbm[:, c * DM + h2 * 512: c * DM + h2 * 512 + 512],
                    a_=stat[:, so_ + c * 2 + h2: so_ + c * 2 + h2 + 1]:
                        v.scalar_tensor_tensor(
                            o, p_, 1.0, xx, OP.mult, OP.add, accum_out=a_),
                    waits=[(ch_m, m_wave), (ch_od2[wl], g_work[wl]),
                           (ch_v, g_stat[wl]), (ch_xc, xcast_done[sg])],
                    inc=ch_v)
            gb[c] = ch_v.n
        gPS["A"] = [("v", gb[2])]
        gPS["B"] = [("v", gb[3])]
        gPS["C"] = [("v", gb[0])]
        gPS["D"] = [("v", gb[1])]
        g["concat"] = ch_m.n
        g["xbm_v"] = ch_v.n

        sview = stat[:, so_:so_ + 8].rearrange("p (c u) -> p c u", u=2)
        for grp in ((2, 3), (0, 1)):
            c0g = grp[0]
            v_s1b = prog.op(
                "vector",
                lambda v, d=stat[:, so_ + 8 + c0g:so_ + 10 + c0g],
                a=sview[:, c0g:c0g + 2, 0], b=sview[:, c0g:c0g + 2, 1]:
                    v.tensor_tensor(d, a, b, OP.add),
                waits=[(ch_v, gb[grp[1]])], inc=ch_v)
            v_nmb = prog.op(
                "vector",
                lambda v, d=stat[:, so_ + 8 + c0g:so_ + 10 + c0g]:
                    v.tensor_scalar(d, d, -1.0 / DM, None, OP.mult),
                waits=[(ch_v, v_s1b)], inc=ch_v)
            sq_last = None
            for c in grp:
                sq_last = prog.op(
                    "scalar",
                    lambda s_, z=work[wl][:, c * DM:(c + 1) * DM],
                    b=stat[:, so_ + 8 + c:so_ + 9 + c],
                    a_=stat[:, so_ + 12 + c:so_ + 13 + c]:
                        s_.activation(ebuf[:], z, AF.Square, bias=b,
                                      accum_out=a_),
                    waits=[(ch_v, v_nmb), (ch_v, g["eb_r"]),
                           (ch_s, g["eb_s"])],
                    inc=ch_s)
                g["eb_s"] = sq_last
            v_lnv = prog.op(
                "scalar",
                lambda s_, d=stat[:, so_ + 16 + c0g:so_ + 18 + c0g],
                z=stat[:, so_ + 12 + c0g:so_ + 14 + c0g]: s_.activation(
                    d, z, AF.Ln, bias=cvap("eps_ln"), scale=cvap("inv_dm")),
                waits=[(ch_s, sq_last)], inc=ch_s)
            v_rst = prog.op(
                "scalar",
                lambda s_, d=stat[:, so_ + 16 + c0g:so_ + 18 + c0g]:
                    s_.activation(d, d, AF.Exp, scale=cvap("neghalf")),
                waits=[(ch_s, v_lnv)], inc=ch_s)
            if c0g == 2:
                trig_s = v_rst
            for c in grp:
                y1vals[c] = prog.op(
                    "vector",
                    lambda v, o=y1b[:, c * DM:(c + 1) * DM],
                    z=work[wl][:, c * DM:(c + 1) * DM],
                    nm=stat[:, so_ + 8 + c:so_ + 9 + c],
                    rs=stat[:, so_ + 16 + c:so_ + 17 + c]:
                        v.tensor_scalar(o, z, nm, rs, OP.add, OP.mult),
                    waits=[(ch_s, v_rst), (ch_yt, g["xbm_yt"])],
                    inc=ch_v)
                g_stat[wl] = ch_v.n
        if a_next_gen is not None:
            next(a_next_gen, None)
            next(a_next_gen, None)

        dummy_val = prog.op(
            "tensor",
            lambda t: t.matmul(C_ps[:, 0:128], bones_sb[:], bones_sb[:],
                               start=True, stop=True),
            waits=[(ch_s, trig_s), (ch_v, gb[0])], inc=ch_m)
        gPS["C"] = gPS["C"] + [("m", dummy_val)]

        def y1t_tr(c):
            dview = y1T.ap().rearrange("p (j t) -> p j t", j=NJ)[
                :, :, c * 128:(c + 1) * 128]
            sview2 = y1b[:, c * DM:(c + 1) * DM].rearrange(
                "p (j f) -> p j f", j=NJ)
            prog.op("sync",
                    lambda s_, d=dview, z=sview2: s_.dma_start(
                        out=d, in_=z, transpose=True),
                    waits=[(ch_v, y1vals[c]), (ch_m, g["y1T"])],
                    inc=ch_yt, inc_n=16)

        for ci in range(2):
            stream_chunk2(sg, "wg", wg_buf, wg_s, wg_in, wg_slot_g,
                          wg_cval, ci)
            stream_chunk2(sg, "wu", wu_buf, wu_s, wu_in, wu_slot_g,
                          wu_cval, ci)
        for c in (2, 3, 0, 1):
            y1t_tr(c)
        stream_chunk2(sg, "wg", wg_buf, wg_s, wg_in, wg_slot_g, wg_cval, 2)
        stream_chunk2(sg, "wu", wu_buf, wu_s, wu_in, wu_slot_g, wu_cval, 2)
        yt_val = ch_yt.n
        g["xbm_yt"] = ch_yt.n
        for mp in range(NMI_F // 2):
            slot = mp % 3
            mi0 = 2 * mp
            m_gu = None
            for which, wbuf_, cvd, ps in ((0, wg_buf, wg_cval, C_ps),
                                          (1, wu_buf, wu_cval, D_ps)):
                pkey = "C" if which == 0 else "D"
                for sub in range(2):
                    woff = slot * 2048 + sub * 1024
                    for ki in range(NJ):
                        lhs = wbuf_[:, woff + ki * 128: woff + (ki + 1) * 128]
                        rhs = y1T[:, ki * ST:(ki + 1) * ST]
                        w = [cvd[mp], (ch_yt, yt_val)]
                        if ki == 0 and sub == 0:
                            w = w + [(chof(t), v) for t, v in gPS[pkey]]
                        fin = (ki == NJ - 1)
                        m_gu = prog.op(
                            "tensor",
                            lambda t, o=ps[:, sub * 512:(sub + 1) * 512],
                            a=lhs, b=rhs, f1=(ki == 0), f2=fin:
                                t.matmul(o, a, b, start=f1, stop=f2),
                            waits=w, inc=ch_m if fin else None, inc_n=NJ)
            wg_slot_g[slot] = ch_m.n
            wu_slot_g[slot] = ch_m.n
            if mp + 3 < 12:
                stream_chunk2(sg, "wg", wg_buf, wg_s, wg_in, wg_slot_g,
                              wg_cval, mp + 3)
                stream_chunk2(sg, "wu", wu_buf, wu_s, wu_in, wu_slot_g,
                              wu_cval, mp + 3)
            v_sg = prog.op("scalar",
                           lambda s_, d=sgb[mp % 2]: s_.activation(
                               d[:], C_ps[:], AF.Silu,
                               scale=cvap("inv127")),
                           waits=[(ch_m, m_gu), (ch_v, g_sg[mp % 2])],
                           inc=ch_s)
            v_h = prog.op("vector",
                          lambda v, o=hbuf[:, mi0 * ST:(mi0 + 2) * ST],
                          sg_=sgb[mp % 2]: v.tensor_tensor(
                              o, D_ps[:], sg_[:], OP.mult),
                          waits=[(ch_s, v_sg), (ch_m, g["h"])], inc=ch_v)
            g_sg[mp % 2] = v_h
            h_vals[mp] = v_h
            gPS["C"] = [("s", v_sg)]
            gPS["D"] = [("v", v_h)]
            if mp in (1, 3, 5, 7, 9) and a_next_gen is not None:
                next(a_next_gen, None)
            if mp in (2, 6):
                pop_den()
        g["y1T"] = ch_m.n
        pop_den()
        if a_next_gen is not None:
            next(a_next_gen, None)

        r2_half = [[None] * 2 for _ in range(NCH)]
        for h2 in range(2):
            pspair = (C_ps, D_ps) if h2 == 0 else (A_ps, B_ps)
            for k in range(min(6, NMI_F)):
                stream_half(sg, "wd", wd_buf, wd_s, wd_in, wd_slot_g,
                            wd_cval, k, h2, k)
            for k in range(NMI_F):
                ridx = k % 6
                for c in range(NCH):
                    ps = pspair[c // 2]
                    half = (c % 2)
                    w = [wd_cval[(k, h2)]]
                    if c == 0:
                        w = w + [(ch_v, h_vals[k // 2])]
                    if k == 0 and c == 0:
                        keys = ("C", "D") if h2 == 0 else ("A", "B")
                        w = w + [
                            (chof(t), v) for key in keys for t, v in gPS[key]]
                    prog.op(
                        "tensor",
                        lambda t, o=ps[:, half * 512:(half + 1) * 512],
                        a=hbuf[:, k * ST + c * 128: k * ST + (c + 1) * 128],
                        b=wd_buf[:, ridx * 512:(ridx + 1) * 512],
                        f1=(k == 0), f2=(k == NMI_F - 1):
                            t.matmul(o, a, b, start=f1, stop=f2),
                        waits=w, inc=ch_m if c == NCH - 1 else None,
                        inc_n=NCH)
                wd_slot_g[ridx] = ch_m.n
                if k + 6 < NMI_F:
                    stream_half(sg, "wd", wd_buf, wd_s, wd_in, wd_slot_g,
                                wd_cval, k + 6, h2, k + 6)
            m_wave = ch_m.n
            for c in range(NCH):
                ps = pspair[c // 2]
                half = (c % 2)
                v_r2 = prog.op(
                    "vector",
                    lambda v, o=work[wl][:, c * DM + h2 * 512:
                                         c * DM + h2 * 512 + 512],
                    p_=ps[:, half * 512:(half + 1) * 512],
                    yy=y1b[:, c * DM + h2 * 512: c * DM + h2 * 512 + 512],
                    a_=stat[:, so_ + c * 2 + h2: so_ + c * 2 + h2 + 1]:
                        v.scalar_tensor_tensor(
                            o, p_, S127_2, yy, OP.mult, OP.add, accum_out=a_),
                    waits=[(ch_m, m_wave), (ch_v, g_stat[wl])], inc=ch_v)
                r2_half[c][h2] = v_r2
            if h2 == 0:
                gPS["C"] = [("v", ch_v.n)]
                gPS["D"] = [("v", ch_v.n)]
                pop_den()
                if a_next_gen is not None:
                    next(a_next_gen, None)
            else:
                gPS["A"] = [("v", ch_v.n)]
                gPS["B"] = [("v", ch_v.n)]
        g["h"] = ch_m.n

        sview = stat[:, so_:so_ + 8].rearrange("p (c u) -> p c u", u=2)
        v_s1b = prog.op("vector",
                        lambda v, d=stat[:, so_ + 8:so_ + 12],
                        a=sview[:, :, 0], b=sview[:, :, 1]:
                            v.tensor_tensor(d, a, b, OP.add),
                        waits=[(ch_v, ch_v.n)], inc=ch_v)
        v_nmb = prog.op("vector",
                        lambda v, d=stat[:, so_ + 8:so_ + 12]:
                            v.tensor_scalar(d, d, -1.0 / DM, None, OP.mult),
                        waits=[(ch_v, v_s1b)], inc=ch_v)
        sq_vals = []
        for c in range(NCH):
            v_sq = prog.op("scalar",
                           lambda s_, z=work[wl][:, c * DM:(c + 1) * DM],
                           b=stat[:, so_ + 8 + c:so_ + 9 + c],
                           a_=stat[:, so_ + 12 + c:so_ + 13 + c]:
                               s_.activation(ebuf[:], z, AF.Square, bias=b,
                                             accum_out=a_),
                           waits=[(ch_v, v_nmb), (ch_v, g["eb_r"]),
                                  (ch_s, g["eb_s"])],
                           inc=ch_s)
            g["eb_s"] = v_sq
            sq_vals.append(v_sq)
        v_lnv = prog.op("scalar",
                        lambda s_, d=stat[:, so_ + 16:so_ + 20],
                        z=stat[:, so_ + 12:so_ + 16]: s_.activation(
                            d, z, AF.Ln, bias=cvap("eps_ln"),
                            scale=cvap("inv_dm")),
                        waits=[(ch_s, sq_vals[-1])], inc=ch_s)
        v_rst = prog.op("scalar",
                        lambda s_, d=stat[:, so_ + 16:so_ + 20]:
                            s_.activation(d, d, AF.Exp,
                                          scale=cvap("neghalf")),
                        waits=[(ch_s, v_lnv)], inc=ch_s)
        for c in range(NCH):
            v_o = prog.op("vector",
                          lambda v, o=work[wl][:, c * DM:(c + 1) * DM],
                          nm=stat[:, so_ + 8 + c:so_ + 9 + c],
                          rs=stat[:, so_ + 16 + c:so_ + 17 + c]:
                              v.tensor_scalar(o, o, nm, rs, OP.add,
                                              OP.mult),
                          waits=[(ch_s, v_rst)], inc=ch_v)
            g_stat[wl] = ch_v.n
        v_out_ready = ch_v.n

        for c in range(NCH):
            for hf in range(2):
                odst = out_d[s, st * ST + c * 128: st * ST + (c + 1) * 128,
                             hf * 512:(hf + 1) * 512]
                zsl = work[wl][:, c * DM + hf * 512: c * DM + (hf + 1) * 512]
                prog.op("scalar",
                        lambda s_, d=odst, z=zsl: s_.dma_start(out=d, in_=z),
                        waits=[(ch_v, v_out_ready)], inc=ch_od2[wl],
                        inc_n=16)
                n_out_dma[0] += 1
        g_work[wl] = ch_od2[wl].n

    emit_conversions()
    emit_xcast(0)
    a_gen = emit_A(0)
    _SENT = object()
    nseg = 0
    while next(a_gen, _SENT) is not _SENT:
        nseg += 1
        if nseg % 2 == 0:
            pop_den()
    while den_q:
        pop_den()
    for sg in range(N_SUBTILES):
        nxt = emit_A(sg + 1) if sg + 1 < N_SUBTILES else None
        emit_BCD(sg, nxt)
        run_gen(nxt)

    prog.op("scalar", lambda s_: s_.nop(),
            waits=[(c, c.n) for c in ch_od2 if c.n])

    block = e(nc.Block())
    prog.replay(block)
    ctx.close()
    return nc


def _prep_weights(w_qkv, w_out, w_gate, w_up, w_down):
    order = []
    for pr in range(4):
        order += [8 + 2 * pr, 9 + 2 * pr, 16 + 2 * pr, 17 + 2 * pr,
                  0 + 2 * pr, 1 + 2 * pr]
    wq = np.empty((3 * NJ, 128, NJ * 128), dtype=np.float32)
    for i, t in enumerate(order):
        wq[i] = (w_qkv[:, t * 128:(t + 1) * 128]
                 .reshape(NJ, 128, 128).transpose(1, 0, 2).reshape(128, -1))
    wo = np.ascontiguousarray(w_out.reshape(NKT, 128, DM), dtype=np.float32)
    wgT = np.ascontiguousarray(w_gate.T)
    wuT = np.ascontiguousarray(w_up.T)
    wdT = np.ascontiguousarray(w_down.T)

    def tile_kmaj(wt):
        a = np.empty((NMI_F, 128, NJ * 128), dtype=np.int32)
        for mi in range(NMI_F):
            a[mi] = (wt[:, mi * 128:(mi + 1) * 128]
                     .reshape(NJ, 128, 128).transpose(1, 0, 2)
                     .reshape(128, -1))
        return a

    wg = tile_kmaj(wgT)
    wu = tile_kmaj(wuT)
    wd = np.ascontiguousarray(wdT.reshape(NMI_F, 128, DM))
    return wq, wo, wg, wu, wd


def make_inputs(L0_attn_x, L0_attn_w_qkv, L0_attn_w_out, L0_attn_alpha,
                L0_attn_beta, L0_ffn_w_gate, L0_ffn_w_up, L0_ffn_w_down,
                ncore=NCORE):
    x = np.ascontiguousarray(np.asarray(L0_attn_x, dtype=np.float32))
    wq, wo, wg, wu, wd = _prep_weights(
        np.asarray(L0_attn_w_qkv, dtype=np.float32),
        np.asarray(L0_attn_w_out, dtype=np.float32),
        np.asarray(L0_ffn_w_gate, dtype=np.int32),
        np.asarray(L0_ffn_w_up, dtype=np.int32),
        np.asarray(L0_ffn_w_down, dtype=np.int32))
    ab = np.concatenate([np.asarray(L0_attn_alpha, dtype=np.float32),
                         np.asarray(L0_attn_beta, dtype=np.float32)])[None, :]
    cv = np.zeros((128, 8), dtype=np.float32)
    for name, (idx, val) in CV.items():
        cv[:, idx] = val
    bones = np.zeros((128, 128), dtype=np.float32)
    bones[0:64, 0:64] = 1.0
    bones[64:128, 64:128] = 1.0
    bones = bones.astype(ml_dtypes.bfloat16)
    ones1 = np.ones((1, 128), dtype=np.float32)
    in_maps = []
    for c in range(ncore):
        in_maps.append({
            "x": np.ascontiguousarray(x[2 * c:2 * c + 2]),
            "wq": wq, "wo": wo, "wg": wg, "wu": wu, "wd": wd,
            "ab": ab, "cv": cv, "bones": bones, "ones1": ones1,
        })
    return in_maps


def kernel(L0_attn_x, L0_attn_w_qkv, L0_attn_w_out, L0_attn_alpha,
           L0_attn_beta, L0_ffn_w_gate, L0_ffn_w_up, L0_ffn_w_down):
    in_maps = make_inputs(L0_attn_x, L0_attn_w_qkv, L0_attn_w_out,
                          L0_attn_alpha, L0_attn_beta, L0_ffn_w_gate,
                          L0_ffn_w_up, L0_ffn_w_down)
    nc = build_nc()
    res = run_bass_kernel_spmd(nc, in_maps, core_ids=list(range(NCORE)))
    out = np.concatenate(
        [np.asarray(r["out"], dtype=np.float32) for r in res.results], axis=0)
    return out.reshape(16, T, DM)
